# version 04
# baseline (speedup 1.0000x reference)
import numpy as np
import ml_dtypes
from dataclasses import dataclass
from contextlib import ExitStack

import concourse.bass as bass
import concourse.mybir as mybir

F32 = mybir.dt.float32
BF16 = mybir.dt.bfloat16
I8 = mybir.dt.int8
AF = mybir.ActivationFunctionType
ALU = mybir.AluOpType
AX = mybir.AxisListType

BF = ml_dtypes.bfloat16


@dataclass
class Cfg:
    D: int = 1024
    E: int = 2048
    N: int = 16
    R: int = 64
    H: int = 4096
    L: int = 8192
    B: int = 2
    K: int = 4
    W1: int = 512
    W2: int = 512
    W3: int = 256

    @property
    def ncores(self): return 8

    @property
    def Tq(self): return self.L * self.B // self.ncores

    @property
    def Eq(self): return self.E // 4


class PB:
    ENGINES = ("sync", "tensor", "scalar", "vector", "gpsimd")

    def __init__(self, nc):
        self.nc = nc
        self.stack = ExitStack()
        self.sems = {}
        self.cnt = {}
        self.streams = {e: [] for e in self.ENGINES}
        self.last_wait = {e: {} for e in self.ENGINES}

    def sem(self, name):
        if name not in self.sems:
            self.sems[name] = self.stack.enter_context(self.nc.semaphore(name))
            self.cnt[name] = 0
        return self.sems[name]

    def emit(self, engine, fn, wait=(), inc=None, inc_by=1):
        incs = []
        if inc is not None:
            incs = inc if isinstance(inc, list) else [(inc, inc_by)]
        for s, _ in incs:
            self.sem(s)
        for s, _ in wait:
            self.sem(s)
        snap = [(s, v) for s, v in wait]

        def op(eng, _snap=snap, _fn=fn, _incs=incs, _eng=engine):
            lw = self.last_wait[_eng]
            for s, v in _snap:
                if v > 0 and lw.get(s, -1) < v:
                    eng.wait_ge(self.sems[s], v)
                    lw[s] = v
            instr = _fn(eng)
            for s, a in _incs:
                assert instr is not None
                instr.then_inc(self.sems[s], a)
        self.streams[engine].append(op)
        for s, a in incs:
            self.cnt[s] += a
        return dict(self.cnt)

    def rowv(self, fn, wait=(), inc=None):
        prev = getattr(self, "_rowv_prev", None)
        w = list(wait) + ([prev] if prev else [])
        sem = inc if inc is not None else "vfl"
        self.emit("vector", fn, wait=w, inc=sem)
        self._rowv_prev = (sem, self.c(sem))

    def init_consts(self, nc_sb, values, vecsem):
        self._ktensor = nc_sb
        self._kmap = {}
        for i, v in enumerate(values):
            self._kmap[float(v)] = i
            self.emit("vector", lambda e, v=v, i=i: e.memset(self._ktensor[:, i:i + 1], float(v)), inc=vecsem)

    def k(self, val, rows=128):
        i = self._kmap[float(val)]
        return self._ktensor[0:rows, i:i + 1]

    def dma(self, engine, out, in_, sem, wait=(), **kw):
        return self.emit(engine, lambda e: e.dma_start(out=out, in_=in_, **kw),
                         wait=wait, inc=sem, inc_by=16)

    def c(self, sem):
        return self.cnt.get(sem, 0)

    def build(self):
        blk = self.stack.enter_context(self.nc.Block())
        for name in self.ENGINES:
            ops = self.streams[name]
            def section(eng, _ops=ops):
                for op in _ops:
                    op(eng)
            getattr(blk, name)(section)


def ceil_div(a, b):
    return -(-a // b)



def build_L1(cfg: Cfg):
    nc = bass.Bass("TRN2", target_bir_lowering=False, debug=False,
                   detect_race_conditions=False)
    D, E, K, Tq, W = cfg.D, cfg.E, cfg.K, cfg.Tq, cfg.W1
    Dk = D // 128
    Em = E // 128
    RN = cfg.R + 2 * cfg.N
    NW = Tq // W

    x_h = nc.dram_tensor("x_sl", [D, Tq], F32, kind="ExternalInput").ap()
    rowd = nc.dram_tensor("rowd", [2, W], F32, kind="Internal").ap()
    xh_h = nc.dram_tensor("x_halo", [D, 128], F32, kind="ExternalInput").ap()
    wxi_h = nc.dram_tensor("WinxT", [D, E], F32, kind="ExternalInput").ap()
    wz_h = nc.dram_tensor("WinzT", [D, E], F32, kind="ExternalInput").ap()
    cw_h = nc.dram_tensor("convw", [128, Em, K], F32, kind="ExternalInput").ap()
    cb_h = nc.dram_tensor("convb", [128, Em], F32, kind="ExternalInput").ap()
    wxp_h = nc.dram_tensor("WxpT", [E, RN], F32, kind="ExternalInput").ap()
    xc_o = nc.dram_tensor("xc_out", [E, Tq], BF16, kind="ExternalOutput").ap()
    z_o = nc.dram_tensor("z_out", [E, Tq], BF16, kind="ExternalOutput").ap()
    xd_o = nc.dram_tensor("xdbl_out", [RN, Tq], F32, kind="ExternalOutput").ap()

    pb = PB(nc)
    st = pb.stack
    sb = lambda name, shape, dt: st.enter_context(nc.sbuf_tensor(name, shape, dt))
    ps = lambda name, shape: st.enter_context(nc.psum_tensor(name, shape, F32))

    wxi = sb("wxi", [128, Dk, E], BF16)
    wz = sb("wz", [128, Dk, E], BF16)
    wxp = sb("wxp", [128, Em, RN], BF16)
    wtmp = sb("wtmp", [128, Dk, 512], F32)
    cw = sb("cw", [128, Em * K], F32)
    cb = sb("cb", [128, Em], F32)
    ones = sb("ones", [128, 1], F32)
    kons = sb("kons", [128, 4], F32)
    xw = sb("xw", [128, Dk, W], F32)
    x2w = sb("x2w", [128, Dk, W], BF16)
    ones_b1 = sb("ones_b1", [128, 1], BF16)
    rowbuf = sb("rowbuf", [1, 6 * W], F32)
    rb = sb("rb", [128, W], F32)
    mrb = sb("mrb", [128, W], F32)
    xn = sb("xn", [128, Dk, W], BF16)
    xnh = sb("xnh", [128, Dk, 128], BF16)
    xi_sb = sb("xi_sb", [128, Em, W + K - 1], F32)
    cv = sb("cv", [128, W], F32)
    xc_sb = sb("xc_sb", [128, Em, W], BF16)
    zw = sb("zw", [128, Em, W], BF16)
    xdw = sb("xdw", [128, W], F32)
    NPOOL = 4
    pool = [ps(f"p{i}", [128, 512]) for i in range(NPOOL)]
    sx_p = ps("sx_p", [1, W])
    sx2_p = ps("sx2_p", [1, W])
    xp_p = ps("xp_p", [RN, W])

    SLD = "sld"
    GST = "gst"
    TMM = "tmm"
    SCA = "sca"
    VEC = "vec"

    for j in range(E // 512):
        pb.dma("sync", wtmp[:], wxi_h.rearrange("(k p) e -> p k e", p=128)[:, :, j * 512:(j + 1) * 512], SLD,
               wait=[(VEC, pb.c(VEC))])
        pb.emit("vector", lambda e, j=j: e.tensor_copy(out=wxi[:, :, j * 512:(j + 1) * 512], in_=wtmp[:]),
                wait=[(SLD, pb.c(SLD))], inc=VEC)
        pb.dma("sync", wtmp[:], wz_h.rearrange("(k p) e -> p k e", p=128)[:, :, j * 512:(j + 1) * 512], SLD,
               wait=[(VEC, pb.c(VEC))])
        pb.emit("vector", lambda e, j=j: e.tensor_copy(out=wz[:, :, j * 512:(j + 1) * 512], in_=wtmp[:]),
                wait=[(SLD, pb.c(SLD))], inc=VEC)
    wxpf = wtmp[:].rearrange("p k e -> p (k e)")[:, 0:Em * RN].rearrange("p (m r) -> p m r", m=Em)
    pb.dma("sync", wxpf, wxp_h.rearrange("(k p) r -> p k r", p=128), SLD,
           wait=[(VEC, pb.c(VEC))])
    pb.emit("vector", lambda e: e.tensor_copy(out=wxp[:], in_=wxpf),
            wait=[(SLD, pb.c(SLD))], inc=VEC)
    pb.dma("sync", cw[:], cw_h.rearrange("p m k -> p (m k)"), SLD)
    pb.dma("sync", cb[:], cb_h[:], SLD)
    pb.emit("vector", lambda e: e.memset(ones[:], 1.0), inc=VEC)
    pb.emit("vector", lambda e: e.memset(ones_b1[:], 1.0), inc=VEC)
    pb.init_consts(kons, [1e-5], VEC)
    wready = [(SLD, pb.c(SLD)), (VEC, pb.c(VEC))]

    def ln_window(x_ap, Wc, xn_out):
        pb.emit("scalar", lambda e: e.activation(out=x2w[:, :, 0:Wc], in_=xw[:, :, 0:Wc], func=AF.Square),
                wait=[(SLD, pb.c(SLD)), (TMM, pb.c(TMM)), (GST, pb.c(GST))], inc=SCA)
        for kt in range(Dk):
            pb.emit("tensor", lambda e, kt=kt: e.matmul(sx_p[:, 0:Wc], ones[:], xw[:, kt, 0:Wc],
                                                        start=(kt == 0), stop=(kt == Dk - 1)),
                    wait=[(SLD, pb.c(SLD)), (VEC, pb.c(VEC))],
                    inc=TMM if kt == Dk - 1 else None)
        for kt in range(Dk):
            pb.emit("tensor", lambda e, kt=kt: e.matmul(sx2_p[:, 0:Wc], ones_b1[:], x2w[:, kt, 0:Wc],
                                                        start=(kt == 0), stop=(kt == Dk - 1)),
                    wait=[(SCA, pb.c(SCA))], inc=TMM if kt == Dk - 1 else None)
        mmdone = pb.c(TMM)
        m_r = rowbuf[:, 0 * W:0 * W + Wc]
        msq_r = rowbuf[:, 1 * W:1 * W + Wc]
        var_r = rowbuf[:, 2 * W:2 * W + Wc]
        r_r = rowbuf[:, 3 * W:3 * W + Wc]
        mr_r = rowbuf[:, 4 * W:4 * W + Wc]
        sd_r = rowbuf[:, 5 * W:5 * W + Wc]
        pb.rowv(lambda e: e.tensor_scalar(out=m_r, in0=sx_p[0:1, 0:Wc], scalar1=1.0 / D, scalar2=None, op0=ALU.mult),
                wait=[(TMM, mmdone), (GST, pb.c(GST))])
        pb.rowv(lambda e: e.tensor_scalar(out=msq_r, in0=sx2_p[0:1, 0:Wc], scalar1=1.0 / D, scalar2=None, op0=ALU.mult))
        pb.rowv(lambda e: e.tensor_tensor(out=var_r, in0=m_r, in1=m_r, op=ALU.mult))
        pb.rowv(lambda e: e.tensor_tensor(out=var_r, in0=msq_r, in1=var_r, op=ALU.subtract), inc=VEC)
        pb.emit("scalar", lambda e: e.activation(out=sd_r, in_=var_r, func=AF.Sqrt, bias=pb.k(1e-5, 1)),
                wait=[(VEC, pb.c(VEC))], inc=SCA)
        pb.rowv(lambda e: e.reciprocal(out=r_r, in_=sd_r), wait=[(SCA, pb.c(SCA))])
        pb.rowv(lambda e: e.tensor_tensor(out=mr_r, in0=m_r, in1=r_r, op=ALU.mult), inc=VEC)
        pb.dma("gpsimd", rowd[0, 0:Wc], r_r, GST, wait=[(VEC, pb.c(VEC))])
        pb.dma("gpsimd", rowd[1, 0:Wc], mr_r, GST)
        pb.dma("gpsimd", rb[:, 0:Wc], rowd[0, 0:Wc][None, :].broadcast_to([128, Wc]), GST,
               wait=[(GST, pb.c(GST))])
        pb.dma("gpsimd", mrb[:, 0:Wc], rowd[1, 0:Wc][None, :].broadcast_to([128, Wc]), GST)
        gw = pb.c(GST)
        for kt in range(Dk):
            pb.emit("vector", lambda e, kt=kt: e.tensor_tensor(out=xn_out[:, kt, 0:Wc], in0=xw[:, kt, 0:Wc], in1=rb[:, 0:Wc], op=ALU.mult),
                    wait=[(GST, gw)])
            pb.emit("vector", lambda e, kt=kt: e.tensor_tensor(out=xn_out[:, kt, 0:Wc], in0=xn_out[:, kt, 0:Wc], in1=mrb[:, 0:Wc], op=ALU.subtract),
                    inc=(VEC if kt == Dk - 1 else None))

    pb.dma("sync", xw[:, :, 0:128], xh_h.rearrange("(k p) t -> p k t", p=128), SLD, wait=wready)
    ln_window(None, 128, xnh)
    halo_evac = {}
    for mt in range(Em):
        pq = pool[mt % NPOOL]
        for kt in range(Dk):
            waits = []
            if kt == 0:
                waits = [(VEC, pb.c(VEC)), (SCA, pb.c(SCA))]
                if mt >= NPOOL:
                    waits.append((SCA, halo_evac[mt - NPOOL]))
            pb.emit("tensor", lambda e, mt=mt, kt=kt, pq=pq: e.matmul(
                pq[:, 0:128], wxi[:, kt, mt * 128:(mt + 1) * 128], xnh[:, kt, :],
                start=(kt == 0), stop=(kt == Dk - 1)),
                wait=waits, inc=TMM if kt == Dk - 1 else None)
        pb.emit("scalar", lambda e, mt=mt, pq=pq: e.activation(out=xi_sb[:, mt, 0:K - 1], in_=pq[:, 129 - K:128], func=AF.Copy),
                wait=[(TMM, pb.c(TMM))], inc=SCA)
        halo_evac[mt] = pb.c(SCA)

    for w in range(NW):
        t0 = w * W
        pb.dma("sync", xw[:], x_h.rearrange("(k p) t -> p k t", p=128)[:, :, t0:t0 + W], SLD,
               wait=[(VEC, pb.c(VEC)), (SCA, pb.c(SCA)), (TMM, pb.c(TMM))])
        ln_window(None, W, xn)
        evac_sca = {}
        for mt in range(2 * Em):
            pq = pool[mt % NPOOL]
            wsrc = wxi if mt < Em else wz
            msl = mt if mt < Em else mt - Em
            for kt in range(Dk):
                waits = []
                if kt == 0:
                    waits.append((VEC, pb.c(VEC)))
                    if mt >= NPOOL:
                        waits.append((SCA, evac_sca[mt - NPOOL]) if (mt - NPOOL) in evac_sca else (VEC, pb.c(VEC)))
                pb.emit("tensor", lambda e, msl=msl, kt=kt, pq=pq, wsrc=wsrc: e.matmul(
                    pq[:, 0:W], wsrc[:, kt, msl * 128:(msl + 1) * 128], xn[:, kt, 0:W],
                    start=(kt == 0), stop=(kt == Dk - 1)),
                    wait=waits, inc=TMM if kt == Dk - 1 else None)
            mmc = pb.c(TMM)
            if mt < Em:
                pb.emit("scalar", lambda e, mt=mt, pq=pq: e.activation(
                    out=xi_sb[:, mt, K - 1:K - 1 + W], in_=pq[:, 0:W], func=AF.Copy),
                    wait=[(TMM, mmc), (VEC, pb.c(VEC))], inc=SCA)
                evac_sca[mt] = pb.c(SCA)
                pb.emit("vector", lambda e, mt=mt: e.tensor_scalar(
                    out=cv[:, 0:W], in0=xi_sb[:, mt, 0:W], scalar1=cw[:, mt * K:mt * K + 1],
                    scalar2=cb[:, mt:mt + 1], op0=ALU.mult, op1=ALU.add),
                    wait=[(SCA, pb.c(SCA))])
                for k in range(1, K):
                    pb.emit("vector", lambda e, mt=mt, k=k: e.scalar_tensor_tensor(
                        out=cv[:, 0:W], in0=xi_sb[:, mt, k:k + W], scalar=cw[:, mt * K + k:mt * K + k + 1],
                        in1=cv[:, 0:W], op0=ALU.mult, op1=ALU.add))
                pb.emit("vector", lambda e, mt=mt: e.tensor_copy(
                    out=xi_sb[:, mt, 0:K - 1], in_=xi_sb[:, mt, W:W + K - 1]), inc=VEC)
                pb.emit("scalar", lambda e, mt=mt: e.activation(out=xc_sb[:, mt, :], in_=cv[:, 0:W], func=AF.Silu),
                        wait=[(VEC, pb.c(VEC))], inc=SCA)
                evac_sca[mt] = pb.c(SCA)
            else:
                pb.emit("scalar", lambda e, msl=msl, pq=pq: e.activation(out=zw[:, msl, :], in_=pq[:, 0:W], func=AF.Copy),
                        wait=[(TMM, mmc)], inc=SCA)
                evac_sca[mt] = pb.c(SCA)
        for kt in range(Em):
            pb.emit("tensor", lambda e, kt=kt: e.matmul(
                xp_p[:, 0:W], wxp[:, kt, :], xc_sb[:, kt, 0:W],
                start=(kt == 0), stop=(kt == Em - 1)),
                wait=[(SCA, pb.c(SCA))] if kt == 0 else (), inc=TMM if kt == Em - 1 else None)
        pb.emit("scalar", lambda e: e.activation(out=xdw[0:RN, 0:W], in_=xp_p[:, 0:W], func=AF.Copy),
                wait=[(TMM, pb.c(TMM))], inc=SCA)
        pb.dma("gpsimd", xc_o[:, t0:t0 + W].rearrange("(m p) t -> p m t", p=128), xc_sb[:], GST,
               wait=[(SCA, pb.c(SCA))])
        pb.dma("gpsimd", z_o[:, t0:t0 + W].rearrange("(m p) t -> p m t", p=128), zw[:], GST)
        pb.dma("gpsimd", xd_o[:, t0:t0 + W], xdw[0:RN, 0:W], GST)
    pb.emit("gpsimd", lambda e: e.engine_nop(), wait=[(GST, pb.c(GST))])
    pb.build()
    return nc


def host_L1_inputs(cfg: Cfg, x, p):
    D, E, K, Tq = cfg.D, cfg.E, cfg.K, cfg.Tq
    Em = E // 128
    WinxT = np.ascontiguousarray(p["in_proj_w"][0:E].T)
    WinzT = np.ascontiguousarray(p["in_proj_w"][E:2 * E].T)
    convw = np.ascontiguousarray(p["conv_w"][:, 0, :].reshape(Em, 128, K).transpose(1, 0, 2))
    convb = np.ascontiguousarray(p["conv_b"].reshape(Em, 128).T)
    WxpT = np.ascontiguousarray(p["x_proj_w"].T)
    maps = []
    nq = cfg.ncores // cfg.B
    for c in range(cfg.ncores):
        b, q = c // nq, c % nq
        xT = np.ascontiguousarray(x[b].T)
        sl = xT[:, q * Tq:(q + 1) * Tq]
        halo = np.zeros((D, 128), np.float32) if q == 0 else xT[:, q * Tq - 128:q * Tq]
        maps.append({
            "x_sl": np.ascontiguousarray(sl), "x_halo": np.ascontiguousarray(halo),
            "WinxT": WinxT, "WinzT": WinzT, "convw": convw, "convb": convb, "WxpT": WxpT,
        })
    return maps



def build_L2(cfg: Cfg):
    nc = bass.Bass("TRN2", target_bir_lowering=False, debug=False,
                   detect_race_conditions=False)
    Eq, L, N, R, W = cfg.Eq, cfg.L, cfg.N, cfg.R, cfg.W2
    Em = Eq // 128
    RN = R + 2 * N
    NW = L // W
    NG = 2
    NH = N // NG

    xc_h = nc.dram_tensor("xc_sl", [Eq, L], BF16, kind="ExternalInput").ap()
    xd_h = nc.dram_tensor("xdbl", [RN, L], F32, kind="ExternalInput").ap()
    wdt_h = nc.dram_tensor("WdtT_sl", [R, Eq], F32, kind="ExternalInput").ap()
    dtb_h = nc.dram_tensor("dtb_sl", [128, Em], F32, kind="ExternalInput").ap()
    al_h = nc.dram_tensor("Alog_sl", [128, Em, N], F32, kind="ExternalInput").ap()
    dp_h = nc.dram_tensor("D_sl", [128, Em], F32, kind="ExternalInput").ap()
    y_o = nc.dram_tensor("y_out", [Eq, L], BF16, kind="ExternalOutput").ap()
    bcd = nc.dram_tensor("bcd", [2, 2, N * W], BF16, kind="Internal").ap()
    import os
    dbg = os.environ.get("L2DBG") == "1"
    if dbg:
        dt_d = nc.dram_tensor("dt_dbg", [128, 4, W], F32, kind="ExternalOutput").ap()
        dA_d = nc.dram_tensor("dA_dbg", [128, 8 * W], F32, kind="ExternalOutput").ap()
        u_d = nc.dram_tensor("u_dbg", [128, 8 * W], BF16, kind="ExternalOutput").ap()
        h_d = nc.dram_tensor("h_dbg", [128, 8 * W], BF16, kind="ExternalOutput").ap()
        bb_d = nc.dram_tensor("bb_dbg", [128, N * W], BF16, kind="ExternalOutput").ap()
        dtlr_d = nc.dram_tensor("dtlr_dbg", [128, L], BF16, kind="ExternalOutput").ap()
        wdt_d = nc.dram_tensor("wdt_dbg", [128, Eq], BF16, kind="ExternalOutput").ap()
        dtb_d = nc.dram_tensor("dtb_dbg", [128, 8], F32, kind="ExternalOutput").ap()
        rawp_d = nc.dram_tensor("rawp_dbg", [128, W], F32, kind="ExternalOutput").ap()
        raw_d = nc.dram_tensor("raw_dbg", [128, W], F32, kind="ExternalOutput").ap()

    pb = PB(nc)
    st = pb.stack
    sb = lambda name, shape, dt: st.enter_context(nc.sbuf_tensor(name, shape, dt))
    ps = lambda name, shape: st.enter_context(nc.psum_tensor(name, shape, F32))

    wdt = sb("wdt", [128, 1, Eq], BF16)
    wdtf = sb("wdtf", [128, 1, Eq], F32)
    dtb = sb("dtb", [128, Em], F32)
    A_sb = sb("A_sb", [128, Em * N], F32)
    D_sb = sb("D_sb", [128, Em], F32)
    CH2 = min(512, L)
    xdstg = sb("xdstg", [R, CH2], F32)
    dtlr = sb("dtlr", [128, L], BF16)
    Bbf = sb("Bbf", [16, L], BF16)
    Cbf = sb("Cbf", [16, L], BF16)
    bstg = sb("bstg", [16, min(512, L)], F32)
    cstg = sb("cstg", [16, min(512, L)], F32)
    xcw = [sb(f"xcw{i}", [128, Em, W], BF16) for i in range(2)]
    Bb = [sb(f"Bb{i}", [128, N * W], BF16) for i in range(2)]
    Cb = [sb(f"Cb{i}", [128, N * W], BF16) for i in range(2)]
    dt_sb = sb("dt_sb", [128, Em, W], F32)
    w_sb = sb("w_sb", [128, W], BF16)
    dA = [sb(f"dA{i}", [128, NH * W], F32) for i in range(2)]
    u_sb = sb("u_sb", [128, NH * W], BF16)
    h_sb = sb("h_sb", [128, NH * W], BF16)
    m_sb = sb("m_sb", [128, NH * W], BF16)
    yh = sb("yh", [128, W], BF16)
    y_sb = sb("y_sb", [128, Em, W], BF16)
    carry = sb("carry", [128, Em * N], F32)
    tmpf = sb("tmpf", [128, W], F32)
    tmpf2 = sb("tmpf2", [128, W], F32)
    kons = sb("kons", [128, 4], F32)
    dt_p = [ps(f"dtp{i}", [128, max(W, 512)]) for i in range(2)]

    SLD, GST, TMM, SCA, VEC = "sld", "gst", "tmm", "sca", "vec"

    pb.init_consts(kons, [1.0], VEC)
    pb.dma("sync", wdtf[0:R, 0, :], wdt_h[0:R, :], SLD)
    pb.emit("vector", lambda e: e.tensor_copy(out=wdt[0:R, 0, :], in_=wdtf[0:R, 0, :]),
            wait=[(SLD, pb.c(SLD))], inc=VEC)
    pb.dma("sync", dtb[:], dtb_h[:], SLD)
    pb.dma("sync", A_sb[:], al_h.rearrange("p m n -> p (m n)"), SLD)
    pb.dma("sync", D_sb[:], dp_h[:], SLD)
    ld0 = pb.c(SLD)
    pb.emit("scalar", lambda e: e.activation(out=A_sb[:], in_=A_sb[:], func=AF.Exp),
            wait=[(SLD, ld0)], inc=SCA)
    pb.emit("vector", lambda e: e.tensor_scalar(out=A_sb[:], in0=A_sb[:], scalar1=-1.0, scalar2=None, op0=ALU.mult),
            wait=[(SCA, pb.c(SCA))], inc=VEC)
    for ch0 in range(0, L, CH2):
        pb.dma("sync", xdstg[:], xd_h[0:R, ch0:ch0 + CH2], SLD, wait=[(VEC, pb.c(VEC))])
        pb.dma("sync", bstg[:], xd_h[R:R + N, ch0:ch0 + CH2], SLD)
        pb.dma("sync", cstg[:], xd_h[R + N:R + 2 * N, ch0:ch0 + CH2], SLD)
        pb.emit("vector", lambda e, ch0=ch0: e.tensor_copy(out=dtlr[0:R, ch0:ch0 + CH2], in_=xdstg[0:R, :]),
                wait=[(SLD, pb.c(SLD))], inc=VEC)
        pb.emit("vector", lambda e, ch0=ch0: e.tensor_copy(out=Bbf[:, ch0:ch0 + CH2], in_=bstg[:, :]), inc=VEC)
        pb.emit("vector", lambda e, ch0=ch0: e.tensor_copy(out=Cbf[:, ch0:ch0 + CH2], in_=cstg[:, :]), inc=VEC)
    pb.emit("vector", lambda e: e.memset(carry[:], 0.0), inc=VEC)
    prep_v = pb.c(VEC)

    last_scan_vec = [prep_v, prep_v]
    for w in range(NW):
        par = w % 2
        t0 = w * W
        pb.dma("sync", xcw[par][:], xc_h.rearrange("(m p) t -> p m t", p=128)[:, :, t0:t0 + W], SLD,
               wait=[(VEC, pb.c(VEC))])
        pb.dma("gpsimd", bcd[par, 0, :], Bbf[:, t0:t0 + W], GST, wait=[(VEC, pb.c(VEC))])
        pb.dma("gpsimd", bcd[par, 1, :], Cbf[:, t0:t0 + W], GST)
        pb.dma("gpsimd", Bb[par][:], bcd[par, 0, :][None, :].broadcast_to([128, N * W]), GST,
               wait=[(GST, pb.c(GST))])
        pb.dma("gpsimd", Cb[par][:], bcd[par, 1, :][None, :].broadcast_to([128, N * W]), GST)
        lded = pb.c(SLD)
        bcded = pb.c(GST)
        for mt in range(Em):
            pq = dt_p[mt % 2]
            pb.emit("tensor", lambda e, mt=mt, pq=pq, t0=t0: e.matmul(
                pq[:, 0:W], wdt[0:R, 0, mt * 128:(mt + 1) * 128], dtlr[0:R, t0:t0 + W],
                start=True, stop=True),
                wait=[(VEC, prep_v), (SCA, pb.c(SCA))], inc=TMM)
            if dbg and w == 0 and mt == 0:
                pb.emit("vector", lambda e, pq=pq: e.tensor_copy(out=tmpf2[:, 0:W], in_=pq[:, 0:W]),
                        wait=[(TMM, pb.c(TMM))], inc=VEC)
                pb.dma("gpsimd", rawp_d[:], tmpf2[:, 0:W], "gst", wait=[(VEC, pb.c(VEC))])
                pb.emit("scalar", lambda e: e.nop(), wait=[("gst", pb.c("gst"))])
                pb.emit("vector", lambda e: e.nop(), wait=[("gst", pb.c("gst"))])
                pb.emit("tensor", lambda e: e.nop(), wait=[("gst", pb.c("gst"))])
                pb.emit("sync", lambda e: e.nop(), wait=[("gst", pb.c("gst"))])
            pb.emit("scalar", lambda e, mt=mt, pq=pq: e.activation(
                out=tmpf[:, 0:W], in_=pq[:, 0:W], func=AF.Exp, bias=dtb[:, mt:mt + 1]),
                wait=[(TMM, pb.c(TMM))], inc=SCA)
            if dbg and w == 0 and mt == 0:
                pb.dma("gpsimd", raw_d[:], tmpf[:, 0:W], "gst", wait=[(SCA, pb.c(SCA))])
                pb.dma("gpsimd", dtlr_d[0:R, :], dtlr[0:R, :], "gst")
                pb.dma("gpsimd", wdt_d[0:R, :], wdt[0:R, 0, :], "gst")
                pb.dma("gpsimd", dtb_d[:, 0:Em], dtb[:, 0:Em], "gst")
                pb.emit("scalar", lambda e: e.nop(), wait=[("gst", pb.c("gst"))])
            pb.emit("scalar", lambda e, mt=mt: e.activation(
                out=dt_sb[:, mt, :], in_=tmpf[:, 0:W], func=AF.Ln, bias=pb.k(1.0)),
                inc=SCA)
        sp_done = pb.c(SCA)
        if dbg and w == 0:
            pb.dma("gpsimd", dt_d[:, 0:Em, :], dt_sb[:], "gst", wait=[(SCA, sp_done)])
            pb.emit("scalar", lambda e: e.nop(), wait=[("gst", pb.c("gst"))])
            pb.emit("vector", lambda e: e.nop(), wait=[("gst", pb.c("gst"))])
        for mt in range(Em):
            pb.emit("vector", lambda e, mt=mt, par=par: e.tensor_tensor(
                out=w_sb[:], in0=dt_sb[:, mt, :], in1=xcw[par][:, mt, :], op=ALU.mult),
                wait=[(SCA, sp_done), (SLD, lded)] + ([(GST, pb.c(GST))] if mt == 0 else []), inc=VEC)
            for ng in range(NG):
                gpar = (w * Em * NG + mt * NG + ng) % 2
                n0 = ng * NH
                for n in range(NH):
                    pb.emit("scalar", lambda e, mt=mt, n=n, n0=n0, gpar=gpar: e.activation(
                        out=dA[gpar][:, n * W:(n + 1) * W], in_=dt_sb[:, mt, :], func=AF.Exp,
                        scale=A_sb[:, mt * N + n0 + n:mt * N + n0 + n + 1]),
                        wait=[(VEC, last_scan_vec[gpar]), (SCA, sp_done)],
                        inc=SCA if n == NH - 1 else None)
                dA_done = pb.c(SCA)
                pb.emit("vector", lambda e, par=par, n0=n0: e.tensor_tensor(
                    out=u_sb[:].rearrange("p (n t) -> p n t", n=NH),
                    in0=w_sb[:, None, :].broadcast_to([128, NH, W]),
                    in1=Bb[par][:, n0 * W:(n0 + NH) * W].rearrange("p (n t) -> p n t", n=NH),
                    op=ALU.mult),
                    wait=[(GST, bcded)], inc=VEC)
                if dbg and w == 0 and mt == 0 and ng == 0:
                    pb.dma("gpsimd", dA_d[:], dA[gpar][:], "gst", wait=[(SCA, pb.c(SCA))])
                    pb.dma("gpsimd", u_d[:], u_sb[:], "gst", wait=[(VEC, pb.c(VEC))])
                    pb.dma("gpsimd", bb_d[:], Bb[par][:], "gst")
                    pb.emit("scalar", lambda e: e.nop(), wait=[("gst", pb.c("gst"))])
                    pb.emit("vector", lambda e: e.nop(), wait=[("gst", pb.c("gst"))])
                for n in range(NH):
                    ci = mt * N + n0 + n
                    pb.emit("vector", lambda e, n=n, ci=ci, gpar=gpar: e.tensor_tensor_scan(
                        out=h_sb[:, n * W:(n + 1) * W], data0=dA[gpar][:, n * W:(n + 1) * W],
                        data1=u_sb[:, n * W:(n + 1) * W], initial=carry[:, ci:ci + 1],
                        op0=ALU.mult, op1=ALU.add),
                        wait=[(SCA, dA_done)], inc=(VEC if n == NH - 1 else None))
                if dbg and w == 0 and mt == 0 and ng == 0:
                    pb.dma("gpsimd", h_d[:], h_sb[:], "gst", wait=[(VEC, pb.c(VEC))])
                    pb.emit("vector", lambda e: e.nop(), wait=[("gst", pb.c("gst"))])
                pb.emit("vector", lambda e, mt=mt, n0=n0: e.tensor_copy(
                    out=carry[:, mt * N + n0:mt * N + n0 + NH],
                    in_=h_sb[:].rearrange("p (n t) -> p n t", n=NH)[:, :, W - 1:W].rearrange("p n o -> p (n o)")))
                pb.emit("vector", lambda e, par=par, n0=n0: e.tensor_tensor(
                    out=m_sb[:], in0=h_sb[:], in1=Cb[par][:, n0 * W:(n0 + NH) * W], op=ALU.mult))
                hw = NH
                while hw > 1:
                    hw //= 2
                    pb.emit("vector", lambda e, hw=hw: e.tensor_tensor(
                        out=m_sb[:, 0:hw * W], in0=m_sb[:, 0:hw * W],
                        in1=m_sb[:, hw * W:2 * hw * W], op=ALU.add))
                if ng == 0:
                    pb.emit("vector", lambda e: e.tensor_copy(out=yh[:], in_=m_sb[:, 0:W]), inc=VEC)
                else:
                    pb.emit("vector", lambda e: e.tensor_tensor(out=yh[:], in0=yh[:], in1=m_sb[:, 0:W], op=ALU.add),
                            inc=VEC)
                last_scan_vec[gpar] = pb.c(VEC)
            pb.emit("vector", lambda e, mt=mt, par=par: e.scalar_tensor_tensor(
                out=y_sb[:, mt, :], in0=xcw[par][:, mt, :], scalar=D_sb[:, mt:mt + 1],
                in1=yh[:], op0=ALU.mult, op1=ALU.add), inc=VEC)
        ydone = pb.c(VEC)
        pb.dma("gpsimd", y_o.rearrange("(m p) t -> p m t", p=128)[:, :, t0:t0 + W], y_sb[:], GST,
               wait=[(VEC, ydone)])
    pb.emit("gpsimd", lambda e: e.engine_nop(), wait=[(GST, pb.c(GST))])
    pb.build()
    return nc


def host_L2_inputs(cfg: Cfg, xc_full, xd_full, p):
    Eq, N, R = cfg.Eq, cfg.N, cfg.R
    Em = Eq // 128
    maps = []
    for c in range(cfg.ncores):
        b, dq = c // 4, c % 4
        rs = slice(dq * Eq, (dq + 1) * Eq)
        WdtT = np.ascontiguousarray(p["dt_proj_w"].T[:, rs])
        dtb = np.ascontiguousarray(p["dt_proj_b"][rs].reshape(Em, 128).T)
        alog = np.ascontiguousarray(p["A_log"][rs].reshape(Em, 128, N).transpose(1, 0, 2))
        dsl = np.ascontiguousarray(p["D_param"][rs].reshape(Em, 128).T)
        maps.append({
            "xc_sl": np.ascontiguousarray(xc_full[b][rs]),
            "xdbl": xd_full[b],
            "WdtT_sl": WdtT, "dtb_sl": dtb, "Alog_sl": alog, "D_sl": dsl,
        })
    return maps



def build_L3(cfg: Cfg):
    nc = bass.Bass("TRN2", target_bir_lowering=False, debug=False,
                   detect_race_conditions=False)
    D, E, H, Tq, W = cfg.D, cfg.E, cfg.H, cfg.Tq, cfg.W3
    Dk, Ek, Hk = D // 128, E // 128, H // 128
    NC = Tq // W
    NT = W // 128

    y_h = nc.dram_tensor("y_full", [E, Tq], BF16, kind="ExternalInput").ap()
    z_h = nc.dram_tensor("z_full", [E, Tq], BF16, kind="ExternalInput").ap()
    x_h = nc.dram_tensor("x_sl", [D, Tq], F32, kind="ExternalInput").ap()
    wo_h = nc.dram_tensor("WoutT", [E, D], F32, kind="ExternalInput").ap()
    w1_h = nc.dram_tensor("w1T", [D, H], F32, kind="ExternalInput").ap()
    w2_h = nc.dram_tensor("w2T", [H, D], F32, kind="ExternalInput").ap()
    b1_h = nc.dram_tensor("b1c", [128, Hk], F32, kind="ExternalInput").ap()
    b2_h = nc.dram_tensor("b2c", [128, Dk], F32, kind="ExternalInput").ap()
    id_h = nc.dram_tensor("ident", [128, 128], F32, kind="ExternalInput").ap()
    out_h = nc.dram_tensor("out_sl", [D, Tq], F32, kind="ExternalOutput").ap()
    wod = nc.dram_tensor("wo_bf_d", [E, D], BF16, kind="Internal").ap()
    q1d = nc.dram_tensor("q1_d", [D, H], BF16, kind="Internal").ap()
    q2d = nc.dram_tensor("q2_d", [H, D], BF16, kind="Internal").ap()
    rowd = nc.dram_tensor("rowd3", [8, W], F32, kind="Internal").ap()
    import os
    dbg = os.environ.get("L3DBG") == "1"
    if dbg:
        x2_d = nc.dram_tensor("x2_dbg", [128, 8, W], F32, kind="ExternalOutput").ap()
        m1_d = nc.dram_tensor("m1_dbg", [128, 8, W], F32, kind="ExternalOutput").ap()
        wb_d = nc.dram_tensor("wb_dbg", [128, 2048], F32, kind="ExternalOutput").ap()
        rows_d = nc.dram_tensor("rows_dbg", [1, 12 * W], F32, kind="ExternalOutput").ap()
        xq_d = nc.dram_tensor("xq_dbg", [128, 8, W], BF16, kind="ExternalOutput").ap()
        g_d = nc.dram_tensor("g_dbg", [128, 16, W], BF16, kind="ExternalOutput").ap()
        gq_d = nc.dram_tensor("gq_dbg", [128, 16, W], BF16, kind="ExternalOutput").ap()
        mw_d = nc.dram_tensor("mw_dbg", [128, 4], F32, kind="ExternalOutput").ap()

    pb = PB(nc)
    st = pb.stack
    sb = lambda name, shape, dt: st.enter_context(nc.sbuf_tensor(name, shape, dt))
    ps = lambda name, shape: st.enter_context(nc.psum_tensor(name, shape, F32))

    SLD, GST, TMM, SCA, VEC = "sld", "gst", "tmm", "sca", "vec"

    wbuf = sb("wbuf", [128, Hk * D], BF16)
    ybuf = sb("ybuf", [128, Ek, W], BF16)
    zbuf = sb("zbuf", [128, Ek, W], BF16)
    xbuf = sb("xbuf", [128, Dk, W], F32)
    x2 = sb("x2", [128, Dk, W], F32)
    g = sb("g", [128, Hk, W], BF16)
    xq = sb("xq", [128, Dk, W], BF16)
    dbig = sb("dbig", [128, Dk, W], F32)
    sq_s = sb("sq_s", [128, W], BF16)
    sq_s2 = sb("sq_s2", [128, W], BF16)
    vb = sb("vb", [128, W], F32)
    cbuf = sb("cbuf", [128, W], F32)
    qb = sb("qb", [128, W], F32)
    r0i = sb("r0i", [128, W], I8)
    r0f = sb("r0f", [128, W], F32)
    r0h = sb("r0h", [128, W], F32)
    obuf = sb("obuf", [128, W], F32)
    ident = sb("id_s", [128, 128], F32)
    ones = sb("ones3", [128, 1], F32)
    ones_bf = sb("onesb3", [128, 1], BF16)
    b1c = sb("b1c_s", [128, Hk], F32)
    b2c = sb("b2c_s", [128, Dk], F32)
    mw1 = sb("mw1", [128, 2], F32)
    mw2 = sb("mw2", [128, 2], F32)
    colacc = sb("colacc", [128, 2], F32)
    rows = sb("rows3", [1, 12 * W], F32)
    c3b = sb("c3b", [128, W], F32)
    c4b = sb("c4b", [128, W], F32)
    rsb = sb("rsb", [128, W], F32)
    tokm = sb("tokm", [128, NT], F32)
    tokr = sb("tokr", [max(NT, 1), 128], F32)
    redx = sb("redx", [128, W], F32)
    kons = sb("kons3", [128, 8], F32)
    NPOOL = 4
    pool = [ps(f"q{i}", [128, 512]) for i in range(NPOOL)]
    sx_p = ps("sx3", [1, W])
    sx2_p = ps("sx23", [1, W])
    tp_p = ps("tp3", [128, 256])

    pb.init_consts(kons, [1e-5, 1e-6, 0.5, -0.5, 1.0], VEC)
    pb.emit("vector", lambda e: e.memset(ones[:], 1.0), inc=VEC)
    pb.emit("vector", lambda e: e.memset(ones_bf[:], 1.0), inc=VEC)
    pb.dma("sync", ident[:], id_h[:], SLD)
    pb.dma("sync", b1c[:], b1_h[:], SLD)
    pb.dma("sync", b2c[:], b2_h[:], SLD)

    def R(i):
        return rows[:, i * W:(i + 1) * W]
    m_r, msq_r, var_r, sd_r, r_r, u_r = R(0), R(1), R(2), R(3), R(4), R(5)
    s_r, rp_r, tk_r, am_r, sx_r, rs_r = R(6), R(7), R(8), R(9), R(10), R(11)
    c3_r, c4_r = R(5), R(6)

    def rowop(fn, wait=(), inc=None):
        pb.rowv(fn, wait=wait, inc=inc)

    def cvt_round(src_ap, dst_ap, scale_b, wait0):
        pb.emit("scalar", lambda e: e.activation(out=r0i[:], in_=src_ap, func=AF.Identity, bias=pb.k(0.5)),
                wait=wait0)
        pb.emit("scalar", lambda e: e.activation(out=r0f[:], in_=r0i[:], func=AF.Copy))
        pb.emit("scalar", lambda e: e.activation(out=r0h[:], in_=r0i[:], func=AF.Identity, bias=pb.k(-0.5)), inc=SCA)
        pb.emit("vector", lambda e: e.tensor_tensor(out=cbuf[:], in0=src_ap, in1=r0h[:], op=ALU.is_lt),
                wait=[(SCA, pb.c(SCA))])
        pb.emit("vector", lambda e: e.scalar_tensor_tensor(out=qb[:], in0=cbuf[:], scalar=-1.0, in1=r0f[:],
                                                           op0=ALU.mult, op1=ALU.add))
        pb.emit("vector", lambda e: e.tensor_tensor(out=dst_ap, in0=qb[:], in1=scale_b, op=ALU.mult), inc=VEC)

    if True:
        wtmp = sb("wtmp3", [128, 4096], F32)
        CH = 4096
        wo_r = wo_h.rearrange("(k p) d -> p k d", p=128)
        wod_r = wod.rearrange("(k p) d -> p k d", p=128)
        for kt in range(Ek):
            pb.dma("sync", wtmp[:, 0:D], wo_r[:, kt, :], SLD, wait=[(GST, pb.c(GST))])
            pb.emit("vector", lambda e, kt=kt: e.tensor_copy(out=wbuf[:, kt * D:(kt + 1) * D], in_=wtmp[:, 0:D]),
                    wait=[(SLD, pb.c(SLD))], inc=VEC)
            pb.dma("gpsimd", wod_r[:, kt, :], wbuf[:, kt * D:(kt + 1) * D], GST, wait=[(VEC, pb.c(VEC))])

        def quant_weight2(w_src, q_dst, nk, ncols, mwcol):
            tot = nk * ncols
            w_r = w_src.rearrange("(k p) d -> p k d", p=128)
            q_r = q_dst.rearrange("(k p) d -> p k d", p=128)
            pb.emit("vector", lambda e: e.memset(colacc[:, 0:1], 0.0), wait=[(GST, pb.c(GST))], inc=VEC)
            for kt in range(nk):
                pb.dma("sync", wtmp[:, 0:ncols], w_r[:, kt, :], SLD, wait=[(VEC, pb.c(VEC))])
                pb.rowv(lambda e, ncols=ncols: e.tensor_reduce(
                    out=colacc[:, 1:2], in_=wtmp[:, 0:ncols], axis=AX.X, op=ALU.add,
                    apply_absolute_value=True), wait=[(SLD, pb.c(SLD))])
                pb.rowv(lambda e: e.tensor_tensor(
                    out=colacc[:, 0:1], in0=colacc[:, 0:1], in1=colacc[:, 1:2], op=ALU.add), inc=VEC)
            pb.emit("tensor", lambda e: e.matmul(sx_p[:, 0:1], ones[:], colacc[:, 0:1],
                                                 start=True, stop=True),
                    wait=[(VEC, pb.c(VEC))], inc=TMM)
            pb.rowv(lambda e, tot=tot: e.tensor_scalar(
                out=rows[:, 0:1], in0=sx_p[0:1, 0:1], scalar1=1.0 / (tot * 128), scalar2=1e-5,
                op0=ALU.mult, op1=ALU.max), wait=[(TMM, pb.c(TMM))])
            pb.rowv(lambda e: e.reciprocal(out=rows[:, 1:2], in_=rows[:, 0:1]), inc=VEC)
            pb.dma("gpsimd", rowd[0, 0:2], rows[0:1, 0:2], GST, wait=[(VEC, pb.c(VEC))])
            pb.dma("gpsimd", mwcol[:, 0:2], rowd[0, 0:2][None, :].broadcast_to([128, 2]), GST,
                   wait=[(GST, pb.c(GST))])
            mwrdy = pb.c(GST)
            for kt in range(nk):
                pb.dma("sync", wtmp[:, 0:ncols], w_r[:, kt, :], SLD,
                       wait=[(GST, pb.c(GST)), (VEC, pb.c(VEC))])
                lded = pb.c(SLD)
                for c0 in range(0, ncols, W):
                    cc = min(W, ncols - c0)
                    qout = sq_s if (c0 // W) % 2 == 0 else sq_s2
                    pb.emit("vector", lambda e, c0=c0, cc=cc, mwcol=mwcol: e.tensor_scalar(
                        out=vb[:, 0:cc], in0=wtmp[:, c0:c0 + cc], scalar1=mwcol[:, 1:2], scalar2=1.0,
                        op0=ALU.mult, op1=ALU.min),
                        wait=[(SLD, lded), (GST, mwrdy), (SCA, pb.c(SCA))])
                    pb.emit("vector", lambda e, cc=cc: e.tensor_scalar(
                        out=vb[:, 0:cc], in0=vb[:, 0:cc], scalar1=-1.0, scalar2=None, op0=ALU.max),
                        inc=VEC)
                    pb.emit("scalar", lambda e, cc=cc: e.activation(
                        out=r0i[:, 0:cc], in_=vb[:, 0:cc], func=AF.Identity, bias=pb.k(0.5)),
                        wait=[(VEC, pb.c(VEC))])
                    pb.emit("scalar", lambda e, cc=cc: e.activation(
                        out=r0f[:, 0:cc], in_=r0i[:, 0:cc], func=AF.Copy))
                    pb.emit("scalar", lambda e, cc=cc: e.activation(
                        out=r0h[:, 0:cc], in_=r0i[:, 0:cc], func=AF.Identity, bias=pb.k(-0.5)),
                        inc=SCA)
                    pb.emit("vector", lambda e, cc=cc: e.tensor_tensor(
                        out=cbuf[:, 0:cc], in0=vb[:, 0:cc], in1=r0h[:, 0:cc], op=ALU.is_lt),
                        wait=[(SCA, pb.c(SCA)), (GST, pb.c(GST) - 32)])
                    pb.emit("vector", lambda e, cc=cc, qout=qout: e.scalar_tensor_tensor(
                        out=qout[:, 0:cc], in0=cbuf[:, 0:cc], scalar=-1.0, in1=r0f[:, 0:cc],
                        op0=ALU.mult, op1=ALU.add), inc=VEC)
                    pb.dma("gpsimd", q_r[:, kt, c0:c0 + cc], qout[:, 0:cc], GST,
                           wait=[(VEC, pb.c(VEC))])

        quant_weight2(w1_h, q1d, Dk, H, mw1)
        if dbg:
            pb.dma("gpsimd", mw_d[:, 0:2], mw1[:, 0:2], "gst", wait=[(GST, pb.c(GST))])
            for en in ("scalar", "vector", "tensor", "sync"):
                pb.emit(en, lambda e: e.nop(), wait=[("gst", pb.c("gst"))])
        quant_weight2(w2_h, q2d, Hk, D, mw2)
        if dbg:
            pb.dma("gpsimd", mw_d[:, 2:4], mw2[:, 0:2], "gst", wait=[(GST, pb.c(GST))])
            pb.dma("gpsimd", rows_d[:, 0:2], rows[0:1, 0:2], "gst")
            for en in ("scalar", "vector", "tensor", "sync"):
                pb.emit(en, lambda e: e.nop(), wait=[("gst", pb.c("gst"))])
    wprep = [(GST, pb.c(GST)), (VEC, pb.c(VEC)), (SCA, pb.c(SCA)), (SLD, pb.c(SLD))]

    wod_f = wod.rearrange("(k p) d -> p k d", p=128)
    q1_f = q1d.rearrange("(k p) d -> p k d", p=128)
    q2_f = q2d.rearrange("(k p) d -> p k d", p=128)

    for tc in range(NC):
        t0 = tc * W
        topw = wprep + [(VEC, pb.c(VEC)), (SCA, pb.c(SCA)), (TMM, pb.c(TMM)), (GST, pb.c(GST))]
        pb.dma("sync", ybuf[:], y_h.rearrange("(m p) t -> p m t", p=128)[:, :, t0:t0 + W], SLD, wait=topw)
        pb.dma("sync", zbuf[:], z_h.rearrange("(m p) t -> p m t", p=128)[:, :, t0:t0 + W], SLD)
        pb.dma("sync", xbuf[:], x_h.rearrange("(k p) t -> p k t", p=128)[:, :, t0:t0 + W], SLD)
        pb.dma("sync", wbuf[:, 0:Ek * D], wod_f[:, :, :], SLD)
        lded = pb.c(SLD)
        if dbg and tc == 0:
            pb.dma("gpsimd", m1_d[:, 0:min(Ek, 8), :], ybuf[:, 0:min(Ek, 8), :], "gst", wait=[(SLD, lded)])
            for en in ("scalar", "vector", "tensor", "sync"):
                pb.emit(en, lambda e: e.nop(), wait=[("gst", pb.c("gst"))])
        pb.emit("scalar", lambda e: e.activation(out=zbuf[:], in_=zbuf[:], func=AF.Silu),
                wait=[(SLD, lded)], inc=SCA)
        pb.emit("vector", lambda e: e.tensor_tensor(out=ybuf[:], in0=ybuf[:], in1=zbuf[:], op=ALU.mult),
                wait=[(SCA, pb.c(SCA))], inc=VEC)
        m1done = pb.c(VEC)
        evac_v = {}
        for mt in range(Dk):
            pq = pool[mt % NPOOL]
            for kt in range(Ek):
                waits = []
                if kt == 0:
                    waits = [(VEC, m1done), (SLD, lded)]
                    if mt >= NPOOL:
                        waits.append((VEC, evac_v[mt - NPOOL]))
                pb.emit("tensor", lambda e, mt=mt, kt=kt, pq=pq: e.matmul(
                    pq[:, 0:W], wbuf[:, kt * D + mt * 128:kt * D + (mt + 1) * 128], ybuf[:, kt, :],
                    start=(kt == 0), stop=(kt == Ek - 1)),
                    wait=waits, inc=TMM if kt == Ek - 1 else None)
            pb.emit("vector", lambda e, mt=mt, pq=pq: e.tensor_tensor(
                out=x2[:, mt, :], in0=pq[:, 0:W], in1=xbuf[:, mt, :], op=ALU.add),
                wait=[(TMM, pb.c(TMM))], inc=VEC)
            evac_v[mt] = pb.c(VEC)
        for kt in range(Dk):
            sqw = [(VEC, evac_v[kt])]
            if kt > 0:
                sqw.append((TMM, pb.c(TMM)))
            pb.emit("scalar", lambda e, kt=kt: e.activation(out=sq_s[:], in_=x2[:, kt, :], func=AF.Square),
                    wait=sqw, inc=SCA)
            pb.emit("tensor", lambda e, kt=kt: e.matmul(sx2_p[:, 0:W], ones_bf[:], sq_s[:],
                                                        start=(kt == 0), stop=(kt == Dk - 1)),
                    wait=[(SCA, pb.c(SCA))], inc=TMM)
        for kt in range(Dk):
            pb.emit("tensor", lambda e, kt=kt: e.matmul(sx_p[:, 0:W], ones[:], x2[:, kt, :],
                                                        start=(kt == 0), stop=(kt == Dk - 1)),
                    wait=[(VEC, evac_v[kt])], inc=TMM if kt == Dk - 1 else None)
        st_done = pb.c(TMM)
        rowop(lambda e: e.tensor_scalar(out=m_r, in0=sx_p[0:1, 0:W], scalar1=1.0 / D, scalar2=None, op0=ALU.mult),
              wait=[(TMM, st_done)])
        rowop(lambda e: e.tensor_scalar(out=msq_r, in0=sx2_p[0:1, 0:W], scalar1=1.0 / D, scalar2=None, op0=ALU.mult))
        rowop(lambda e: e.tensor_tensor(out=var_r, in0=m_r, in1=m_r, op=ALU.mult))
        rowop(lambda e: e.tensor_tensor(out=var_r, in0=msq_r, in1=var_r, op=ALU.subtract), inc=VEC)
        pb.emit("scalar", lambda e: e.activation(out=sd_r, in_=var_r, func=AF.Sqrt, bias=pb.k(1e-5, 1)),
                wait=[(VEC, pb.c(VEC))], inc=SCA)
        rowop(lambda e: e.reciprocal(out=r_r, in_=sd_r), wait=[(SCA, pb.c(SCA))])
        rowop(lambda e: e.tensor_tensor(out=u_r, in0=var_r, in1=r_r, op=ALU.mult))
        rowop(lambda e: e.tensor_tensor(out=u_r, in0=u_r, in1=r_r, op=ALU.mult), inc=VEC)
        pb.emit("scalar", lambda e: e.activation(out=s_r, in_=u_r, func=AF.Sqrt, bias=pb.k(1e-6, 1)),
                wait=[(VEC, pb.c(VEC))], inc=SCA)
        rowop(lambda e: e.reciprocal(out=s_r, in_=s_r), wait=[(SCA, pb.c(SCA))])
        rowop(lambda e: e.tensor_tensor(out=rp_r, in0=r_r, in1=s_r, op=ALU.mult), inc=VEC)
        rp_done = pb.c(VEC)
        pb.dma("gpsimd", rowd[2, :], m_r, GST, wait=[(VEC, rp_done)])
        pb.dma("gpsimd", c4b[:], rowd[2, :][None, :].broadcast_to([128, W]), GST, wait=[(GST, pb.c(GST))])
        mb_rdy = pb.c(GST)
        for kt in range(Dk):
            pb.emit("vector", lambda e, kt=kt: e.tensor_tensor(
                out=dbig[:, kt, :], in0=x2[:, kt, :], in1=c4b[:], op=ALU.subtract),
                wait=[(GST, mb_rdy)] if kt == 0 else ())
        pb.emit("vector", lambda e: e.tensor_reduce(
            out=redx[:], in_=dbig[:].rearrange("p m t -> p t m"), axis=AX.X, op=ALU.max,
            apply_absolute_value=True), inc=VEC)

        def amax_to_row(dst_row, wait0):
            for j in range(NT):
                pb.emit("tensor", lambda e, j=j: e.transpose(
                    tp_p[0:128, 0:128], redx[:, j * 128:(j + 1) * 128], ident[:]),
                    wait=(wait0 if j == 0 else [(VEC, pb.c(VEC))]), inc=TMM)
                pb.emit("vector", lambda e, j=j: e.tensor_reduce(
                    out=tokm[:, j:j + 1], in_=tp_p[0:128, 0:128], axis=AX.X, op=ALU.max),
                    wait=[(TMM, pb.c(TMM))], inc=VEC)
            pb.emit("tensor", lambda e: e.transpose(tp_p[0:NT, 0:128], tokm[:, 0:NT], ident[:]),
                    wait=[(VEC, pb.c(VEC))], inc=TMM)
            pb.emit("vector", lambda e: e.tensor_copy(out=tokr[0:NT, :], in_=tp_p[0:NT, 0:128]),
                    wait=[(TMM, pb.c(TMM)), (GST, pb.c(GST))], inc=VEC)
            pb.dma("gpsimd", rowd[3, :], tokr[0:NT, :], GST, wait=[(VEC, pb.c(VEC))])
            pb.dma("gpsimd", dst_row, rowd[3, :][None, :], GST, wait=[(GST, pb.c(GST))])
            return pb.c(GST)

        tk_rdy = amax_to_row(tk_r, [(VEC, pb.c(VEC))])
        rowop(lambda e: e.tensor_tensor(out=am_r, in0=tk_r, in1=rp_r, op=ALU.mult), wait=[(GST, tk_rdy)])
        rowop(lambda e: e.tensor_scalar(out=am_r, in0=am_r, scalar1=1e-5, scalar2=None, op0=ALU.max))
        rowop(lambda e: e.reciprocal(out=sx_r, in_=am_r))
        rowop(lambda e: e.tensor_scalar(out=sx_r, in0=sx_r, scalar1=127.0, scalar2=None, op0=ALU.mult))
        rowop(lambda e: e.tensor_scalar(out=rs_r, in0=am_r, scalar1=1.0 / 127.0, scalar2=None, op0=ALU.mult))
        rowop(lambda e: e.tensor_tensor(out=c3_r, in0=rp_r, in1=sx_r, op=ALU.mult))
        rowop(lambda e: e.tensor_tensor(out=c4_r, in0=m_r, in1=c3_r, op=ALU.mult), inc=VEC)
        pb.dma("gpsimd", rowd[4, :], c3_r, GST, wait=[(VEC, pb.c(VEC))])
        pb.dma("gpsimd", rowd[5, :], c4_r, GST)
        pb.dma("gpsimd", rowd[6, :], rs_r, GST)
        pb.dma("gpsimd", c3b[:], rowd[4, :][None, :].broadcast_to([128, W]), GST, wait=[(GST, pb.c(GST))])
        pb.dma("gpsimd", c4b[:], rowd[5, :][None, :].broadcast_to([128, W]), GST)
        pb.dma("gpsimd", rsb[:], rowd[6, :][None, :].broadcast_to([128, W]), GST)
        cb_rdy = pb.c(GST)
        for kt in range(Dk):
            pb.emit("vector", lambda e, kt=kt: e.tensor_tensor(out=vb[:], in0=x2[:, kt, :], in1=c3b[:], op=ALU.mult),
                    wait=[(GST, cb_rdy), (SCA, pb.c(SCA))])
            pb.emit("vector", lambda e: e.tensor_tensor(out=vb[:], in0=vb[:], in1=c4b[:], op=ALU.subtract), inc=VEC)
            cvt_round(vb[:], xq[:, kt, :], rsb[:], [(VEC, pb.c(VEC))])
        xq_done = pb.c(VEC)
        if dbg and tc == 0:
            pb.dma("gpsimd", x2_d[:, 0:Dk, :], x2[:], "gst", wait=[(VEC, pb.c(VEC))])
            pb.dma("gpsimd", xq_d[:, 0:Dk, :], xq[:], "gst")
            pb.dma("gpsimd", rows_d[:], rows[:], "gst")
            for en in ("scalar", "vector", "tensor", "sync"):
                pb.emit(en, lambda e: e.nop(), wait=[("gst", pb.c("gst"))])
        pb.dma("sync", wbuf[:, 0:Dk * H], q1_f[:, :, :], SLD, wait=[(TMM, st_done)])
        q1_lded = pb.c(SLD)
        if dbg and tc == 0:
            pb.dma("gpsimd", wb_d[:, 0:2048], wbuf[:, 0:2048], "gst", wait=[(SLD, q1_lded)])
            for en in ("scalar", "vector", "tensor", "sync"):
                pb.emit(en, lambda e: e.nop(), wait=[("gst", pb.c("gst"))])
        ev1 = {}
        for mh in range(Hk):
            pq = pool[mh % NPOOL]
            for kt in range(Dk):
                waits = []
                if kt == 0:
                    waits = [(SLD, q1_lded), (VEC, xq_done)]
                    if mh >= NPOOL:
                        waits.append((SCA, ev1[mh - NPOOL]))
                pb.emit("tensor", lambda e, mh=mh, kt=kt, pq=pq: e.matmul(
                    pq[:, 0:W], wbuf[:, kt * H + mh * 128:kt * H + (mh + 1) * 128], xq[:, kt, :],
                    start=(kt == 0), stop=(kt == Dk - 1)),
                    wait=waits, inc=TMM if kt == Dk - 1 else None)
            pb.emit("scalar", lambda e, mh=mh, pq=pq: e.activation(
                out=g[:, mh, :], in_=pq[:, 0:W], func=AF.Gelu, scale=mw1[:, 0:1], bias=b1c[:, mh:mh + 1]),
                wait=[(TMM, pb.c(TMM))], inc=SCA)
            ev1[mh] = pb.c(SCA)
        if dbg and tc == 0:
            pb.dma("gpsimd", g_d[:, 0:Hk, :], g[:, 0:Hk, :], "gst", wait=[(SCA, pb.c(SCA))])
            for en in ("scalar", "vector", "tensor", "sync"):
                pb.emit(en, lambda e: e.nop(), wait=[("gst", pb.c("gst"))])
        for mh in range(Hk):
            sqw = [] if mh == 0 else [(TMM, pb.c(TMM))]
            pb.emit("scalar", lambda e, mh=mh: e.activation(out=sq_s[:], in_=g[:, mh, :], func=AF.Square),
                    wait=sqw, inc=SCA)
            pb.emit("tensor", lambda e, mh=mh: e.matmul(sx2_p[:, 0:W], ones_bf[:], sq_s[:],
                                                        start=(mh == 0), stop=(mh == Hk - 1)),
                    wait=[(SCA, pb.c(SCA))], inc=TMM)
        g2_done = pb.c(TMM)
        pb.emit("vector", lambda e: e.tensor_reduce(
            out=redx[:], in_=g[:].rearrange("p m t -> p t m"), axis=AX.X, op=ALU.max,
            apply_absolute_value=True), wait=[(SCA, ev1[Hk - 1])], inc=VEC)
        tk2_rdy = amax_to_row(tk_r, [(VEC, pb.c(VEC))])
        rowop(lambda e: e.tensor_scalar(out=msq_r, in0=sx2_p[0:1, 0:W], scalar1=1.0 / H, scalar2=None, op0=ALU.mult),
              wait=[(TMM, g2_done), (GST, tk2_rdy)], inc=VEC)
        pb.emit("scalar", lambda e: e.activation(out=sd_r, in_=msq_r, func=AF.Sqrt, bias=pb.k(1e-6, 1)),
                wait=[(VEC, pb.c(VEC))], inc=SCA)
        rowop(lambda e: e.reciprocal(out=r_r, in_=sd_r), wait=[(SCA, pb.c(SCA))])
        rowop(lambda e: e.tensor_tensor(out=am_r, in0=tk_r, in1=r_r, op=ALU.mult))
        rowop(lambda e: e.tensor_scalar(out=am_r, in0=am_r, scalar1=1e-5, scalar2=None, op0=ALU.max))
        rowop(lambda e: e.reciprocal(out=sx_r, in_=am_r))
        rowop(lambda e: e.tensor_scalar(out=sx_r, in0=sx_r, scalar1=127.0, scalar2=None, op0=ALU.mult))
        rowop(lambda e: e.tensor_scalar(out=rs_r, in0=am_r, scalar1=1.0 / 127.0, scalar2=None, op0=ALU.mult))
        rowop(lambda e: e.tensor_tensor(out=c3_r, in0=r_r, in1=sx_r, op=ALU.mult), inc=VEC)
        pb.dma("gpsimd", rowd[4, :], c3_r, GST, wait=[(VEC, pb.c(VEC))])
        pb.dma("gpsimd", rowd[6, :], rs_r, GST)
        pb.dma("gpsimd", c3b[:], rowd[4, :][None, :].broadcast_to([128, W]), GST, wait=[(GST, pb.c(GST))])
        pb.dma("gpsimd", rsb[:], rowd[6, :][None, :].broadcast_to([128, W]), GST)
        c5_rdy = pb.c(GST)
        def gq_tile(i):
            return ybuf[:, i, :] if i < Ek else zbuf[:, i - Ek, :]
        for kt2 in range(Hk):
            pb.emit("vector", lambda e, kt2=kt2: e.tensor_tensor(out=vb[:], in0=g[:, kt2, :], in1=c3b[:], op=ALU.mult),
                    wait=[(GST, c5_rdy), (SCA, pb.c(SCA))], inc=VEC)
            cvt_round(vb[:], gq_tile(kt2), rsb[:], [(VEC, pb.c(VEC))])
        gq_done = pb.c(VEC)
        if dbg and tc == 0:
            for i in range(Hk):
                pb.dma("gpsimd", gq_d[:, i, :], (ybuf[:, i, :] if i < Ek else zbuf[:, i - Ek, :]), "gst",
                       wait=[(VEC, gq_done)])
            for en in ("scalar", "vector", "tensor", "sync"):
                pb.emit(en, lambda e: e.nop(), wait=[("gst", pb.c("gst"))])
        pb.dma("sync", wbuf[:, 0:Hk * D], q2_f[:, :, :], SLD, wait=[(TMM, pb.c(TMM))])
        q2_lded = pb.c(SLD)
        for half in range(Dk // NPOOL):
            for kt2 in range(Hk):
                for i2 in range(NPOOL):
                    m2 = half * NPOOL + i2
                    waits = []
                    if kt2 == 0 and i2 == 0:
                        waits = [(SLD, q2_lded), (VEC, gq_done), (SCA, pb.c(SCA))]
                    pb.emit("tensor", lambda e, kt2=kt2, m2=m2, i2=i2: e.matmul(
                        pool[i2][:, 0:W], wbuf[:, kt2 * D + m2 * 128:kt2 * D + (m2 + 1) * 128],
                        gq_tile(kt2), start=(kt2 == 0), stop=(kt2 == Hk - 1)),
                        wait=waits, inc=TMM if (kt2 == Hk - 1 and i2 == NPOOL - 1) else None)
            mm2_done = pb.c(TMM)
            for i2 in range(NPOOL):
                m2 = half * NPOOL + i2
                pb.emit("scalar", lambda e, m2=m2, i2=i2: e.activation(
                    out=obuf[:], in_=pool[i2][:, 0:W], func=AF.Identity, scale=mw2[:, 0:1],
                    bias=b2c[:, m2:m2 + 1]),
                    wait=[(TMM, mm2_done), (VEC, pb.c(VEC)), (GST, pb.c(GST))], inc=SCA)
                pb.emit("vector", lambda e, m2=m2: e.tensor_tensor(
                    out=obuf[:], in0=obuf[:], in1=x2[:, m2, :], op=ALU.add),
                    wait=[(SCA, pb.c(SCA))], inc=VEC)
                pb.dma("gpsimd", out_h.rearrange("(k p) t -> p k t", p=128)[:, m2, t0:t0 + W], obuf[:], GST,
                       wait=[(VEC, pb.c(VEC))])
    pb.emit("gpsimd", lambda e: e.nop(), wait=[(GST, pb.c(GST))])
    pb.build()
    return nc


def host_L3_inputs(cfg: Cfg, y_full, z_full, x, p):
    D, E, H, Tq = cfg.D, cfg.E, cfg.H, cfg.Tq
    Dk, Hk = D // 128, H // 128
    WoutT = np.ascontiguousarray(p["out_proj_w"].T)
    w1T = np.ascontiguousarray(p["w1"].T)
    w2T = np.ascontiguousarray(p["w2"].T)
    b1c = np.ascontiguousarray(p["b1"].reshape(Hk, 128).T)
    b2c = np.ascontiguousarray(p["b2"].reshape(Dk, 128).T)
    ident = np.eye(128, dtype=np.float32)
    maps = []
    nq = cfg.ncores // cfg.B
    for c in range(cfg.ncores):
        b, q = c // nq, c % nq
        xT = np.ascontiguousarray(x[b].T)
        maps.append({
            "y_full": np.ascontiguousarray(y_full[b][:, q * Tq:(q + 1) * Tq]),
            "z_full": np.ascontiguousarray(z_full[b][:, q * Tq:(q + 1) * Tq]),
            "x_sl": np.ascontiguousarray(xT[:, q * Tq:(q + 1) * Tq]),
            "WoutT": WoutT, "w1T": w1T, "w2T": w2T, "b1c": b1c, "b2c": b2c, "ident": ident,
        })
    return maps


from concourse.bass_utils import run_bass_kernel_spmd

_CFG = Cfg()


def kernel(L0_attn_x, L0_attn_in_proj_w, L0_attn_conv_w, L0_attn_conv_b, L0_attn_x_proj_w,
           L0_attn_dt_proj_w, L0_attn_dt_proj_b, L0_attn_A_log, L0_attn_D_param, L0_attn_out_proj_w,
           L0_ffn_w1, L0_ffn_b1, L0_ffn_w2, L0_ffn_b2):
    cfg = _CFG
    p = {
        "in_proj_w": np.asarray(L0_attn_in_proj_w), "conv_w": np.asarray(L0_attn_conv_w),
        "conv_b": np.asarray(L0_attn_conv_b), "x_proj_w": np.asarray(L0_attn_x_proj_w),
        "dt_proj_w": np.asarray(L0_attn_dt_proj_w), "dt_proj_b": np.asarray(L0_attn_dt_proj_b),
        "A_log": np.asarray(L0_attn_A_log), "D_param": np.asarray(L0_attn_D_param),
        "out_proj_w": np.asarray(L0_attn_out_proj_w),
        "w1": np.asarray(L0_ffn_w1), "b1": np.asarray(L0_ffn_b1),
        "w2": np.asarray(L0_ffn_w2), "b2": np.asarray(L0_ffn_b2),
    }
    x = np.asarray(L0_attn_x)
    B, L, D = x.shape
    E, Tq = cfg.E, cfg.Tq
    nq = cfg.ncores // cfg.B

    nc1 = build_L1(cfg)
    maps1 = host_L1_inputs(cfg, x, p)
    r1 = run_bass_kernel_spmd(nc1, maps1, core_ids=list(range(cfg.ncores)))
    xc_full = [np.concatenate([r1.results[b * nq + q]["xc_out"] for q in range(nq)], axis=1)
               for b in range(B)]
    z_full = [np.concatenate([r1.results[b * nq + q]["z_out"] for q in range(nq)], axis=1)
              for b in range(B)]
    xd_full = [np.concatenate([r1.results[b * nq + q]["xdbl_out"] for q in range(nq)], axis=1)
               for b in range(B)]

    nc2 = build_L2(cfg)
    maps2 = host_L2_inputs(cfg, xc_full, xd_full, p)
    r2 = run_bass_kernel_spmd(nc2, maps2, core_ids=list(range(cfg.ncores)))
    y_full = [np.concatenate([r2.results[b * 4 + dq]["y_out"] for dq in range(4)], axis=0)
              for b in range(B)]

    nc3 = build_L3(cfg)
    maps3 = host_L3_inputs(cfg, y_full, z_full, x, p)
    r3 = run_bass_kernel_spmd(nc3, maps3, core_ids=list(range(cfg.ncores)))
    out = np.empty((B, L, D), dtype=np.float32)
    for c in range(cfg.ncores):
        b, q = c // nq, c % nq
        out[b, q * Tq:(q + 1) * Tq, :] = r3.results[c]["out_sl"].T
    return out
